# version 25
# baseline (speedup 1.0000x reference)
import numpy as np
import ml_dtypes
import concourse.bass as bass
import concourse.mybir as mybir
from concourse.bass_utils import run_bass_kernel_spmd
from concourse.alu_op_type import AluOpType

F32 = mybir.dt.float32
BF16 = mybir.dt.bfloat16
AFT = mybir.ActivationFunctionType
AX = mybir.AxisListType

NB = 2
SEQ = 2048
D = 1024
DC = 8
NT = 4
NQ = 16

class _Op:
    __slots__ = ("stream", "chan", "fn", "dma", "pos", "chan_idx", "marked",
                 "waits", "drain", "clock", "tag")

class Sched:
    def __init__(self, streams, total_chans=()):
        self.streams = {s: [] for s in streams}
        self.total_chans = set(total_chans)
        self.chan_count = {}
        self.res = {}
        self.stream_clock = {s: {} for s in streams}
        self.last_drain = {s: -1 for s in streams}

    def add(self, stream, fn, reads=(), writes=(), chan=None, dma=False, tag=""):
        chan = chan or stream
        op = _Op()
        op.stream, op.chan, op.fn, op.dma, op.tag = stream, chan, fn, dma, tag
        op.pos = len(self.streams[stream])
        op.chan_idx = self.chan_count.get(chan, 0)
        self.chan_count[chan] = op.chan_idx + 1
        op.marked = dma
        op.waits = {}
        op.drain = False
        deps = []
        for k in reads:
            st = self.res.get(k)
            if st and st[0] is not None:
                deps.append((st[0], False))
        for k in writes:
            st = self.res.get(k)
            if st:
                if st[0] is not None:
                    deps.append((st[0], False))
                for r in st[1].values():
                    deps.append((r, True))
        clock = dict(self.stream_clock[stream])
        for dep, is_war in deps:
            if dep is op:
                continue
            if dep.dma:
                if dep.chan in self.total_chans:
                    idx = self.chan_count[dep.chan] - 1
                else:
                    idx = dep.chan_idx
                if clock.get(dep.chan, -1) < idx:
                    op.waits[dep.chan] = max(op.waits.get(dep.chan, -1), idx)
                    for c2, i2 in dep.clock.items():
                        if clock.get(c2, -1) < i2:
                            clock[c2] = i2
                    clock[dep.chan] = idx
            elif dep.stream == stream:
                if stream != "T":
                    if clock.get(dep.chan, -1) < dep.chan_idx:
                        dep.marked = True
                        op.waits[dep.chan] = max(op.waits.get(dep.chan, -1), dep.chan_idx)
                        clock[dep.chan] = dep.chan_idx
            else:
                if clock.get(dep.chan, -1) < dep.chan_idx:
                    dep.marked = True
                    op.waits[dep.chan] = max(op.waits.get(dep.chan, -1), dep.chan_idx)
                    for c2, i2 in dep.clock.items():
                        if clock.get(c2, -1) < i2:
                            clock[c2] = i2
                    clock[dep.chan] = dep.chan_idx
        op.clock = clock
        self.stream_clock[stream] = clock
        for k in reads:
            self.res.setdefault(k, [None, {}])[1][stream] = op
        for k in writes:
            self.res[k] = [op, {}]
        self.streams[stream].append(op)
        return op

    def finalize(self):
        self.mark_val = {}
        per_chan_ops = {}
        for s, ops in self.streams.items():
            for o in ops:
                per_chan_ops.setdefault(o.chan, []).append(o)
        for c, ops in per_chan_ops.items():
            ops.sort(key=lambda o: o.chan_idx)
            acc = 0
            vals = []
            for o in ops:
                if o.marked:
                    acc += 16 if o.dma else 1
                vals.append(acc)
            self.mark_val[c] = vals

    def wait_value(self, chan, idx):
        return self.mark_val[chan][idx]

    def emit(self, stream, eng, sems):
        for o in self.streams[stream]:
            if o.drain:
                eng.drain()
            for c, idx in sorted(o.waits.items(), key=lambda kv: str(kv[0])):
                v = self.wait_value(c, idx)
                if v > 0:
                    eng.wait_ge(sems[c], v)
            inst = o.fn(eng)
            if o.marked:
                if inst is None:
                    inst = eng.nop()
                inst.then_inc(sems[o.chan], 16 if o.dma else 1)


DO_MIX = True

def mix(*lists):
    out = []
    if not DO_MIX:
        for L in lists:
            out.extend(L)
        return out
    tot = [max(sum(c for c, _ in L), 1) for L in lists]
    done = [0.0] * len(lists)
    idx = [0] * len(lists)
    while True:
        best, bi = None, -1
        for i, L in enumerate(lists):
            if idx[i] < len(L):
                f = done[i] / tot[i]
                if best is None or f < best:
                    best, bi = f, i
        if bi < 0:
            break
        c, fn = lists[bi][idx[bi]]
        idx[bi] += 1
        done[bi] += c
        out.append((c, fn))
    return out

def play(units):
    for _, fn in units:
        fn()


def build_program():
    nc = bass.Bass("TRN2", target_bir_lowering=False)
    from contextlib import ExitStack
    es = ExitStack()

    x = nc.dram_tensor("x", [NB, SEQ, D], F32, kind="ExternalInput")
    W1 = nc.dram_tensor("W1", [128, 8], F32, kind="ExternalInput")
    b1 = nc.dram_tensor("b1", [128, 8], F32, kind="ExternalInput")
    W2 = nc.dram_tensor("W2", [2 * D, D], F32, kind="ExternalInput")
    U1 = nc.dram_tensor("U1", [D, 256], F32, kind="ExternalInput")
    V1 = nc.dram_tensor("V1", [256, 4096], F32, kind="ExternalInput")
    fb1 = nc.dram_tensor("fb1", [128, 32], F32, kind="ExternalInput")
    U2 = nc.dram_tensor("U2", [4096, 256], F32, kind="ExternalInput")
    V2 = nc.dram_tensor("V2", [256, D], F32, kind="ExternalInput")
    b2 = nc.dram_tensor("b2", [1, 1024], F32, kind="ExternalInput")
    idb = nc.dram_tensor("idb", [128, 128], BF16, kind="ExternalInput")
    idf = nc.dram_tensor("idf", [128, 128], F32, kind="ExternalInput")
    out = nc.dram_tensor("out", [NB, SEQ, D], F32, kind="ExternalOutput")

    def sb(name, shape, dt=F32):
        return es.enter_context(nc.sbuf_tensor(name, shape, dt))

    def ps(name, shape, dt=F32):
        return es.enter_context(nc.psum_tensor(name, shape, dt))

    w2sb = sb("w2sb", [128, 16 * 1024], BF16)
    u1sb = sb("u1sb", [128, 8 * 256], BF16)
    v1sb = sb("v1sb", [128, 2 * 4096], BF16)
    u2sb = sb("u2sb", [128, 32 * 256], BF16)
    v2sb = sb("v2sb", [128, 2 * 1024], BF16)
    wstage = [sb(f"wstage{i}", [128, 2048]) for i in range(2)]
    w1c = sb("w1c", [128, 8])
    b1c = sb("b1c", [128, 8])
    fb1c = sb("fb1c", [128, 32])
    b2bf = sb("b2bf", [1, 1024], BF16)
    onesf = sb("onesf", [1, 128])
    onesb = sb("onesb", [128, 1], BF16)
    onesbr = sb("onesbr", [1, 128], BF16)
    identbf = sb("identbf", [128, 128], BF16)
    identf = sb("identf", [128, 128])

    nfm = sb("nfm", [128, 8 * 2048], BF16)
    xa = [sb(f"xa{i}", [128, 1024]) for i in range(2)]
    xr = [sb(f"xr{i}", [128, 1024]) for i in range(2)]
    outsb = [sb(f"outsb{i}", [128, 1024]) for i in range(2)]
    xnfm = [sb(f"xnfm{i}", [128, 8 * 512], BF16) for i in range(2)]
    ntok = [sb(f"ntok{i}", [128, 1024], BF16) for i in range(2)]
    xntok = [sb(f"xntok{i}", [128, 1024], BF16) for i in range(4)]
    pfm = [sb(f"pfm{i}", [128, 2 * 512], BF16) for i in range(2)]
    qfm = [sb(f"qfm{i}", [128, 2 * 512], BF16) for i in range(2)]
    hbf = [sb(f"hbf{i}", [128, 512], BF16) for i in range(2)]
    e512 = [wstage[1][:, 0:512], wstage[1][:, 512:1024]]
    vjunk = sb("vjunk", [128, 1536], BF16)
    cpart = sb("cpart16", [128, 16])
    call = sb("call", [128, 16])
    crow4 = [sb(f"crow4{i}", [4, 128]) for i in range(2)]
    crow1 = sb("crow1", [1, 2048])
    cscr = sb("cscr", [1, 512])
    cols = {}
    for n in ["sum", "ssq", "mean", "var", "u", "ru", "sq", "rs", "rs2",
              "scaleA", "biasA", "invs", "biasB", "t1", "t2"]:
        cols[n] = [sb(f"c_{n}{p}", [128, 16]) for p in range(2)]
    sq32 = [sb(f"sq32_{p}", [128, 32]) for p in range(2)]
    iv32 = [sb(f"iv32_{p}", [128, 32]) for p in range(2)]

    s0 = ps("s0", [128, 512]); s1 = ps("s1", [128, 512])
    s2 = ps("s2", [128, 512])
    s3 = ps("s3", [128, 512]); hx = ps("hx", [128, 512])
    q0 = ps("q0", [128, 512]); q1 = ps("q1", [128, 512])
    tb = ps("tb0", [128, 1024], BF16)
    pool = [s0, s1]

    streams = ["T", "V", "S", "G", "SY"]
    dma_chans = (["QC"] + [("QW", i) for i in range(2)] + [("QXA", i) for i in range(2)]
                 + [("QXR", i) for i in range(2)] + [("QXO", i) for i in range(2)]
                 + [("QXS", i) for i in range(2)]
                 + [("QXF", i) for i in range(2)]
                 + [("QTN", i) for i in range(2)]
                 + [("QST", i) for i in range(3)])
    sch = Sched(streams, total_chans=["QC"])
    sems = {}
    for c in streams + dma_chans:
        nm = c if isinstance(c, str) else c[0] + str(c[1])
        sems[c] = es.enter_context(nc.semaphore("sem_" + nm))

    A = sch.add
    state = {"pool": 0, "vj": 0}

    def pool_slot():
        i = state["pool"] % 2
        state["pool"] += 1
        return pool[i], ("ps", i)

    def vj_slot():
        i = state["vj"] % 3
        state["vj"] += 1
        return i

    A("SY", lambda e: e.dma_start(out=identbf[:, :], in_=idb[:, :]),
      writes=[("identbf",)], chan="QC", dma=True)
    A("SY", lambda e: e.dma_start(out=identf[:, :], in_=idf[:, :]),
      writes=[("identf",)], chan="QC", dma=True)
    A("SY", lambda e: e.dma_start(out=w1c[:, :], in_=W1[:, :]),
      writes=[("w1c",)], chan="QC", dma=True)
    A("SY", lambda e: e.dma_start(out=b1c[:, :], in_=b1[:, :]),
      writes=[("b1c",)], chan="QC", dma=True)
    A("SY", lambda e: e.dma_start(out=fb1c[:, :], in_=fb1[:, :]),
      writes=[("fb1c",)], chan="QC", dma=True)
    A("SY", lambda e: e.dma_start(out=crow1[0:1, 0:1024], in_=b2[0:1, :]),
      writes=[("crow1", 0), ("crow1", 1)], chan="QC", dma=True)
    A("V", lambda e: e.tensor_copy(b2bf[0:1, :], crow1[0:1, 0:1024]),
      reads=[("crow1", 0), ("crow1", 1)], writes=[("b2bf",)])
    A("V", lambda e: e.memset(onesf[:, :], 1.0), writes=[("onesf",)])
    A("V", lambda e: e.memset(onesb[:, :], 1.0), writes=[("onesb",)])
    A("V", lambda e: e.memset(onesbr[:, :], 1.0), writes=[("onesbr",)])

    def weight_units():
        chunks = []
        u1v = U1.rearrange("(c p) r -> p c r", p=128)
        chunks.append((lambda: u1v[:, :, :], u1sb, 0, 2048, "u1", "S"))
        v1v = V1.rearrange("(c p) f -> p c f", p=128)
        for j in range(4):
            c, hh = j // 2, j % 2
            chunks.append((lambda c=c, hh=hh: v1v[:, c:c + 1, hh * 2048:(hh + 1) * 2048],
                           v1sb, c * 4096 + hh * 2048, 2048, "v1", "S"))
        u2v = U2.rearrange("(c p) r -> p c r", p=128)
        for j in range(4):
            chunks.append((lambda j=j: u2v[:, 8 * j:8 * (j + 1), :], u2sb,
                           j * 2048, 2048, "u2", "S"))
        v2v = V2.rearrange("(c p) d -> p c d", p=128)
        chunks.append((lambda: v2v[:, :, :], v2sb, 0, 2048, "v2", "S"))
        w2v = W2.rearrange("(c p) d -> p c d", p=128)
        for j in range(8):
            chunks.append((lambda j=j: w2v[:, 2 * j:2 * (j + 1), :], w2sb,
                           j * 2048, 2048, "w2", "S"))
        units = []
        for i, (apf, dest, off, n, wname, eng) in enumerate(chunks):
            stg = wstage[i % 2]
            eng = "S" if i % 2 == 0 else "V"
            def emit(apf=apf, stg=stg, i=i, dest=dest, off=off, n=n, wname=wname, eng=eng):
                A("SY", lambda e: e.dma_start(out=stg[:, :], in_=apf()),
                  writes=[("wstage", i % 2)], chan=("QW", i % 2), dma=True)
                if eng == "S":
                    A("S", lambda e: e.copy(dest[:, off:off + n], stg[:, :]),
                      reads=[("wstage", i % 2)], writes=[("w", wname)])
                else:
                    A("V", lambda e: e.tensor_copy(dest[:, off:off + n], stg[:, :]),
                      reads=[("wstage", i % 2)], writes=[("w", wname)])
            units.append((4000, emit))
        return units

    def emit_colmath(b, c0, c1):
        p = b % 2
        C = {n: cols[n][p][:, c0:c1] for n in cols}
        K = lambda n: ("cm", p, n, c0)
        rd_sum = [("c_sum", p, st) for st in range(c0, c1)]
        rd_ssq = [("c_ssq", p, st) for st in range(c0, c1)]
        def vop(fn, reads, writes):
            A("V", fn, reads=reads, writes=writes)
        vop(lambda e: e.tensor_scalar(out=C["mean"], in0=C["sum"],
            scalar1=1.0 / D, scalar2=None, op0=AluOpType.mult), rd_sum, [K("mean")])
        vop(lambda e: e.tensor_scalar(out=C["t1"], in0=C["ssq"],
            scalar1=1.0 / D, scalar2=None, op0=AluOpType.mult), rd_ssq, [K("t1")])
        vop(lambda e: e.tensor_tensor(out=C["t2"], in0=C["mean"],
            in1=C["mean"], op=AluOpType.mult), [K("mean")], [K("t2")])
        vop(lambda e: e.tensor_tensor(out=C["var"], in0=C["t1"],
            in1=C["t2"], op=AluOpType.subtract), [K("t1"), K("t2")], [K("var")])
        vop(lambda e: e.tensor_scalar(out=C["u"], in0=C["var"],
            scalar1=1e-5, scalar2=None, op0=AluOpType.add), [K("var")], [K("u")])
        vop(lambda e: e.reciprocal(C["ru"], C["u"]), [K("u")], [K("ru")])
        vop(lambda e: e.tensor_tensor(out=C["t1"], in0=C["var"],
            in1=C["ru"], op=AluOpType.mult), [K("var"), K("ru")], [K("t1")])
        vop(lambda e: e.tensor_scalar(out=C["sq"], in0=C["t1"],
            scalar1=float(D), scalar2=1e-12, op0=AluOpType.mult, op1=AluOpType.max),
            [K("t1")], [K("sq")])
        vop(lambda e: e.reciprocal(C["rs"], C["sq"]), [K("sq")], [K("rs")])
        sqa = sq32[p][:, c0:c1]; sqb = sq32[p][:, 16 + c0:16 + c1]
        iva = iv32[p][:, c0:c1]; ivb = iv32[p][:, 16 + c0:16 + c1]
        vop(lambda e: e.tensor_copy(sqa, C["ru"]), [K("ru")], [K("s32a")])
        vop(lambda e: e.tensor_copy(sqb, C["rs"]), [K("rs")], [K("s32b")])
        A("S", lambda e: e.activation(out=iva, in_=sqa, func=AFT.Sqrt),
          reads=[K("s32a")], writes=[K("iva")])
        A("S", lambda e: e.activation(out=ivb, in_=sqb, func=AFT.Sqrt),
          reads=[K("s32b")], writes=[K("ivb")])
        def newton(ycols, ykey, acol, akey, t1, outc, outk):
            vop(lambda e: e.tensor_tensor(out=t1, in0=ycols, in1=ycols,
                op=AluOpType.mult), [ykey], [K(outk + "t1")])
            vop(lambda e: e.tensor_tensor(out=t1, in0=t1, in1=acol,
                op=AluOpType.mult), [K(outk + "t1"), akey], [K(outk + "t1")])
            vop(lambda e: e.tensor_scalar(out=t1, in0=t1, scalar1=-0.5,
                scalar2=1.5, op0=AluOpType.mult, op1=AluOpType.add),
                [K(outk + "t1")], [K(outk + "t1")])
            vop(lambda e: e.tensor_tensor(out=outc, in0=ycols, in1=t1,
                op=AluOpType.mult), [ykey, K(outk + "t1")], [K(outk)])
        newton(iva, K("iva"), C["u"], K("u"), C["t1"], C["invs"], "invs")
        newton(ivb, K("ivb"), C["sq"], K("sq"), C["t2"], C["rs2"], "rs2")
        vop(lambda e: e.tensor_tensor(out=C["scaleA"], in0=C["invs"],
            in1=C["rs2"], op=AluOpType.mult),
            [K("invs"), K("rs2")], [K("scaleA")])
        vop(lambda e: e.scalar_tensor_tensor(out=C["biasA"], in0=C["mean"],
            scalar=-1.0, in1=C["scaleA"], op0=AluOpType.mult, op1=AluOpType.mult),
            [K("mean"), K("scaleA")], [K("biasA")] + [("cmA", p, st) for st in range(c0, c1)])
        vop(lambda e: e.scalar_tensor_tensor(out=C["biasB"], in0=C["mean"],
            scalar=-1.0, in1=C["invs"], op0=AluOpType.mult, op1=AluOpType.mult),
            [K("mean"), K("invs")], [K("biasB")] + [("cmB", p, st) for st in range(c0, c1)])

    def tp_group(src, skey, dst, dkey, st):
        def emit():
            for dc in range(DC):
                A("T", lambda e, dc=dc:
                  e.transpose(tb[:, dc * 128:(dc + 1) * 128],
                              src[:, dc * 128:(dc + 1) * 128], identbf[:, :]),
                  reads=[skey, ("identbf",)], writes=[("ptb",)])
            A("V", lambda e:
              e.tensor_copy(dst.rearrange("p (c f) -> p c f", c=8)[:, :,
                                                                  st * 128:(st + 1) * 128],
                            tb[:, :].rearrange("p (c f) -> p c f", c=8)),
              reads=[("ptb",)], writes=[dkey])
        return (1600, emit)

    def ph1_units2(b, st, slot_buf, slot_key, load_chan=None, eng="V", tok=None):
        p = b % 2
        C = cols
        tbuf, tkey = tok if tok is not None else (ntok[st % 2], ("ntok", st % 2))
        lc = []
        if load_chan is not None:
            def ld():
                A("SY", lambda e: e.dma_start(out=slot_buf[:, :],
                                              in_=x[b, st * 128:(st + 1) * 128, :]),
                  writes=[slot_key], chan=load_chan, dma=True)
            lc.append((600, ld))
        def cast():
            if eng == "V":
                A("V", lambda e: e.tensor_scalar(out=tbuf[:, :], in0=slot_buf[:, :],
                                scalar1=C["scaleA"][p][:, st:st + 1],
                                scalar2=C["biasA"][p][:, st:st + 1],
                                op0=AluOpType.mult, op1=AluOpType.add),
                  reads=[slot_key, ("cmA", p, st)], writes=[tkey])
            else:
                A("S", lambda e: e.activation(out=tbuf[:, :], in_=slot_buf[:, :],
                                func=AFT.Identity, scale=C["scaleA"][p][:, st:st + 1],
                                bias=C["biasA"][p][:, st:st + 1]),
                  reads=[slot_key, ("cmA", p, st)], writes=[tkey])
        lc.append((600, cast))
        t = st // 4
        return lc, [tp_group(tbuf, tkey, nfm, ("nfm", t), st)]

    def ph1_units(b, st, slot_buf, slot_key, load_chan=None, eng="V", tok=None):
        lc, tp = ph1_units2(b, st, slot_buf, slot_key, load_chan, eng, tok)
        return lc + tp

    def prep_units2(b, st, slot_buf, slot_key, load_chan=None, load_eng="G"):
        p = b % 2
        C = cols
        t = st // 4
        lc = []
        if load_chan is not None:
            def ld():
                A(load_eng, lambda e: e.dma_start(out=slot_buf[:, :],
                                                  in_=x[b, st * 128:(st + 1) * 128, :]),
                  writes=[slot_key], chan=load_chan, dma=True)
            lc.append((600, ld))
        def cast():
            A("V", lambda e: e.tensor_scalar(out=xntok[st % 4][:, :], in0=slot_buf[:, :],
                            scalar1=C["invs"][p][:, st:st + 1],
                            scalar2=C["biasB"][p][:, st:st + 1],
                            op0=AluOpType.mult, op1=AluOpType.add),
              reads=[slot_key, ("cmB", p, st)], writes=[("xntok", st % 4)])
        lc.append((600, cast))
        return lc, [tp_group(xntok[st % 4], ("xntok", st % 4), xnfm[t % 2],
                             ("xnfm", t % 2), st % 4)]

    def prep_units(b, st, slot_buf, slot_key, load_chan=None, load_eng="G"):
        lc, tp = prep_units2(b, st, slot_buf, slot_key, load_chan, load_eng)
        return lc + tp

    def stats_units(b, st, slot_buf, slot_key, load_chan):
        p = b % 2
        units = []
        def ld():
            A("SY", lambda e: e.dma_start(out=slot_buf[:, :],
                                          in_=x[b, st * 128:(st + 1) * 128, :]),
              writes=[slot_key], chan=load_chan, dma=True)
        def do_sum():
            A("V", lambda e: e.tensor_scalar(out=ntok[0][:, :], in0=slot_buf[:, :],
                            scalar1=0.0, scalar2=None, op0=AluOpType.add,
                            op1=AluOpType.add,
                            accum_out=cols["sum"][p][:, st:st + 1]),
              reads=[slot_key], writes=[("c_sum", p, st), ("ntok", 0)])
        def do_ssq():
            A("S", lambda e: e.activation(out=ntok[1][:, :], in_=slot_buf[:, :],
                            func=AFT.Square,
                            accum_out=cols["ssq"][p][:, st:st + 1]),
              reads=[slot_key], writes=[("c_ssq", p, st), ("ntok", 1)])
        units.append((2500, ld))
        units.append((1200, do_sum))
        units.append((1200, do_ssq))
        return units

    def sim_block_unit(b, q, kc, j, nb_, last):
        def emit():
            bank, bkey = pool_slot()
            vs = vj_slot()
            state["blkvs", b, q, kc] = vs
            for dc in range(DC):
                A("T", lambda e, dc=dc:
                  e.matmul(bank[:, :],
                           nfm[:, dc * 2048 + q * 128: dc * 2048 + (q + 1) * 128],
                           nfm[:, dc * 2048 + kc * 512: dc * 2048 + (kc + 1) * 512],
                           start=(dc == 0), stop=(dc == DC - 1)),
                  reads=[("nfm", q // 4), ("nfm", kc)],
                  writes=[bkey], tag=f"sim{b}_{q}_{kc}")
            col = 4 * (q % 4) + j
            A("V", lambda e:
              e.tensor_scalar(out=vjunk[:, vs * 512:(vs + 1) * 512],
                              in0=bank[:, :], scalar1=0.0,
                              scalar2=None, op0=AluOpType.max, op1=AluOpType.add,
                              accum_out=cpart[:, col:col + 1]),
              reads=[bkey], writes=[("cp", col), ("vj", vs)])
            if last:
                A("V", lambda e:
                  e.reduce_sum(call[:, q:q + 1], cpart[:, 4 * (q % 4):4 * (q % 4) + nb_],
                               axis=AX.X),
                  reads=[("cp", 4 * (q % 4) + jj) for jj in range(nb_)],
                  writes=[("call", q)])
        return (4400, emit)

    def cs_unit(b, q, kc):
        def emit():
            vs = state["blkvs", b, q, kc]
            bank, bkey = pool_slot()
            A("T", lambda e:
              e.matmul(bank[0:1, 0:512], onesb[:, :], vjunk[:, vs * 512:(vs + 1) * 512],
                       start=True, stop=True),
              reads=[("vj", vs), ("onesb",)], writes=[bkey], tag=f"cs{b}_{q}_{kc}")
            A("V", lambda e:
              e.tensor_tensor(out=crow1[0:1, kc * 512:(kc + 1) * 512],
                              in0=crow1[0:1, kc * 512:(kc + 1) * 512],
                              in1=bank[0:1, 0:512], op=AluOpType.add),
              reads=[bkey, ("crow1", kc)], writes=[("crow1", kc)])
        return (700, emit)

    def sim_tile_units(b, t):
        units = []
        pend = []
        if t == 0:
            blocks = [(q, kc, kc, 4, kc == 3) for kc in range(4) for q in range(4)]
        else:
            blocks = [(q, kc, kc - t, 4 - t, kc == 3)
                      for q in range(4 * t, 4 * t + 4) for kc in range(t, 4)]
        for (q, kc, j, nb_, last) in blocks:
            units.append(sim_block_unit(b, q, kc, j, nb_, last))
            if pend:
                units.append(pend.pop(0))
            if kc > q // 4:
                pend.append(cs_unit(b, q, kc))
        units.extend(pend)
        units.append(crow_unit(b, t))
        return units

    def crow_unit(b, t):
        def emit():
            bank, bkey = pool_slot()
            A("T", lambda e: e.transpose(bank[0:4, 0:128], call[:, 4 * t:4 * t + 4],
                                         identf[:, :]),
              reads=[("call", 4 * t + i) for i in range(4)] + [("identf",)],
              writes=[bkey])
            A("V", lambda e: e.tensor_copy(crow4[t % 2][:, :], bank[0:4, 0:128]),
              reads=[bkey], writes=[("crow4", t % 2)])
            for qi in range(4):
                A("G", lambda e, qi=qi:
                  e.dma_start(out=cscr[0:1, qi * 128:(qi + 1) * 128],
                              in_=crow4[t % 2][qi:qi + 1, :]),
                  reads=[("crow4", t % 2)], writes=[("cscr",)],
                  chan=("QTN", t % 2), dma=True)
            A("V", lambda e:
              e.tensor_tensor(out=crow1[0:1, t * 512:(t + 1) * 512],
                              in0=crow1[0:1, t * 512:(t + 1) * 512],
                              in1=cscr[0:1, 0:512], op=AluOpType.add),
              reads=[("cscr",), ("crow1", t)], writes=[("crow1", t)])
        return (300, emit)

    def cf_unit(b, t):
        def emit():
            A("T", lambda e: e.matmul(s2[:, :], onesf[:, :],
                                      crow1[0:1, t * 512:(t + 1) * 512],
                                      start=True, stop=True),
              reads=[("onesf",), ("crow1", t)], writes=[("ps2",)], tag=f"T2_{b}_{t}")
            for dc in range(DC):
                A("S", lambda e, dc=dc:
                  e.activation(out=e512[dc % 2][:, :], in_=s2[:, :], func=AFT.Exp,
                               scale=w1c[:, dc:dc + 1], bias=b1c[:, dc:dc + 1]),
                  reads=[("ps2",), ("w1c",), ("b1c",)],
                  writes=[("e512", dc % 2), ("wstage", 1)])
                A("S", lambda e, dc=dc:
                  e.activation(out=nfm[:, dc * 2048 + t * 512: dc * 2048 + (t + 1) * 512],
                               in_=e512[dc % 2][:, :], func=AFT.Ln, bias=1.0, scale=1.0),
                  reads=[("e512", dc % 2), ("wstage", 1)], writes=[("nfm", t)],
                  tag=f"cf{b}_{t}_{dc}")
        return (2600, emit)

    def crow_memset_unit(b):
        def emit():
            A("V", lambda e: e.memset(crow1[:, :], 0.0),
              writes=[("crow1", tt) for tt in range(NT)])
        return (0, emit)

    def ffn_units(b, t):
        units = []
        def P(rc):
            def emit():
                bank = s3 if rc == 0 else hx
                bkey = ("ps3",) if rc == 0 else ("pshx",)
                for dc in range(DC):
                    A("T", lambda e, dc=dc:
                      e.matmul(bank[:, :],
                               u1sb[:, dc * 256 + rc * 128: dc * 256 + (rc + 1) * 128],
                               xnfm[t % 2][:, dc * 512:(dc + 1) * 512],
                               start=(dc == 0), stop=(dc == DC - 1)),
                      reads=[("xnfm", t % 2), ("w", "u1")],
                      writes=[bkey], tag=f"P{b}_{t}")
                A("V", lambda e: e.tensor_copy(pfm[t % 2][:, rc * 512:(rc + 1) * 512],
                                               bank[:, :]),
                  reads=[bkey], writes=[("pfm", t % 2)])
            return (4400, emit)
        units.append(P(0))
        units.append(P(1))
        def H(fc):
            bank = s3 if fc % 2 == 0 else hx
            bkey = ("ps3",) if fc % 2 == 0 else ("pshx",)
            for c in range(2):
                A("T", lambda e, c=c:
                  e.matmul(bank[:, :], v1sb[:, c * 4096 + fc * 128:c * 4096 + (fc + 1) * 128],
                           pfm[t % 2][:, c * 512:(c + 1) * 512],
                           start=(c == 0), stop=(c == 1)),
                  reads=[("pfm", t % 2), ("w", "v1")],
                  writes=[bkey], tag=f"H{b}_{t}_{fc}")
            A("S", lambda e:
              e.activation(out=hbf[fc % 2][:, :], in_=bank[:, :], func=AFT.Gelu_apprx_tanh,
                           scale=1.0, bias=fb1c[:, fc:fc + 1]),
              reads=[bkey, ("fb1c",)], writes=[("hbf", fc % 2)])
        def Q(fc):
            for rc in range(2):
                A("T", lambda e, rc=rc:
                  e.matmul([q0, q1][rc][:, :],
                           u2sb[:, fc * 256 + rc * 128: fc * 256 + (rc + 1) * 128],
                           hbf[fc % 2][:, :], start=(fc == 0), stop=(fc == 31)),
                  reads=[("hbf", fc % 2), ("w", "u2")],
                  writes=[("pq0",) if rc == 0 else ("pq1",)], tag=f"Q{b}_{t}")
        def HQ(fc):
            def emit():
                H(fc)
                if fc > 0:
                    Q(fc - 1)
            return (2300, emit)
        for fc in range(32):
            units.append(HQ(fc))
        def Qlast():
            Q(31)
            for rc in range(2):
                A("V", lambda e, rc=rc:
                  e.tensor_copy(qfm[t % 2][:, rc * 512:(rc + 1) * 512], [q0, q1][rc][:, :]),
                  reads=[("pq0",) if rc == 0 else ("pq1",)], writes=[("qfm", t % 2)])
        units.append((1200, Qlast))
        return units

    def fin_units(b, t):
        units = []
        for s_ in range(4):
            st = 4 * t + s_
            def gx(st=st):
                A("G", lambda e: e.dma_start(out=outsb[st % 2][:, :],
                                             in_=x[b, st * 128:(st + 1) * 128, :]),
                  writes=[("outsb", st % 2)], chan=("QXF", st % 2), dma=True)
            units.append((600, gx))
            for h in range(2):
                def finh(s_=s_, st=st, h=h):
                    bank, bkey = pool_slot()
                    A("T", lambda e:
                      e.matmul(bank[:, :], onesbr[:, :],
                               b2bf[0:1, h * 512:(h + 1) * 512], start=True, stop=False),
                      reads=[("onesbr",), ("b2bf",)], writes=[bkey], tag=f"Fb2_{b}_{t}_{s_}")
                    for dc in range(DC):
                        A("T", lambda e, dc=dc:
                          e.matmul(bank[:, :],
                                   xnfm[t % 2][:, dc * 512 + s_ * 128: dc * 512 + (s_ + 1) * 128],
                                   w2sb[:, dc * 1024 + h * 512: dc * 1024 + h * 512 + 512],
                                   start=False, stop=False),
                          reads=[("xnfm", t % 2), ("w", "w2")],
                          writes=[bkey], tag=f"F{b}_{t}_{s_}")
                    for dc in range(DC):
                        A("T", lambda e, dc=dc:
                          e.matmul(bank[:, :],
                                   nfm[:, dc * 2048 + t * 512 + s_ * 128: dc * 2048 + t * 512 + (s_ + 1) * 128],
                                   w2sb[:, (8 + dc) * 1024 + h * 512: (8 + dc) * 1024 + h * 512 + 512],
                                   start=False, stop=False),
                          reads=[("nfm", t), ("w", "w2")], writes=[bkey])
                    for rc in range(2):
                        A("T", lambda e, rc=rc:
                          e.matmul(bank[:, :],
                                   qfm[t % 2][:, rc * 512 + s_ * 128: rc * 512 + (s_ + 1) * 128],
                                   v2sb[:, rc * 1024 + h * 512: rc * 1024 + h * 512 + 512],
                                   start=False, stop=(rc == 1)),
                          reads=[("qfm", t % 2), ("w", "v2")], writes=[bkey])
                    A("V", lambda e:
                      e.tensor_tensor(out=outsb[st % 2][:, h * 512:(h + 1) * 512],
                                      in0=bank[:, :],
                                      in1=outsb[st % 2][:, h * 512:(h + 1) * 512],
                                      op=AluOpType.add),
                      reads=[bkey, ("outsb", st % 2)], writes=[("outsb", st % 2)])
                units.append((9600, finh))
            def store(st=st):
                A("G", lambda e: e.dma_start(out=out[b, st * 128:(st + 1) * 128, :],
                                             in_=outsb[st % 2][:, :]),
                  reads=[("outsb", st % 2)], chan=("QST", st % 2), dma=True)
            units.append((0, store))
        return units

    def slot6(st):
        i = st % 6
        if i < 2:
            return xa[i], ("xa", i), ("QXA", i)
        if i < 4:
            return xr[i - 2], ("xr", i - 2), ("QXS", i - 2)
        return outsb[i - 4], ("outsb", i - 4), ("QXO", i - 4)

    startup = []
    startup.append(crow_memset_unit(0))
    for st in range(6):
        buf, key, chn = slot6(st)
        startup.extend(stats_units(0, st, buf, key, chn)[:1])
    for g in range(NT):
        for s_ in range(4):
            st = 4 * g + s_
            buf, key, chn = slot6(st)
            startup.extend(stats_units(0, st, buf, key, chn)[1:])
        startup.append((0, lambda g=g: emit_colmath(0, 4 * g, 4 * g + 4)))
        for s_ in range(4):
            st = 4 * g + s_
            buf, key, chn = slot6(st)
            startup.extend(ph1_units(0, st, buf, key, load_chan=None, eng="S"))
            if g == 0:
                startup.extend(prep_units(0, st, buf, key, load_chan=None))
            if st + 6 < 16:
                b2_, k2_, c2_ = slot6(st + 6)
                startup.extend(stats_units(0, st + 6, b2_, k2_, c2_)[:1])
        grp = []
        for q in range(4):
            grp.append(sim_block_unit(0, q, g, g, 4, g == 3))
            if g > 0:
                grp.append(cs_unit(0, q, g))
        startup.extend(grp)
    startup.append(crow_unit(0, 0))
    wunits = weight_units()

    play(startup)
    play(wunits[:6])
    wrest = wunits[6:] + [(sum(c for c, _ in wunits[6:]), lambda: None)]

    def batch_steps(b):
        nxt = b + 1
        for sg in range(6):
            if sg == 0:
                continue
            Lsim = sim_tile_units(b, sg) if 1 <= sg <= 3 else []
            if 1 <= sg <= 4:
                Lsim.insert(min(1, len(Lsim)), cf_unit(b, sg - 1))
            Lffn = ffn_units(b, sg - 1) if 1 <= sg <= 4 else []
            Lfin = fin_units(b, sg - 2) if 2 <= sg <= 4 else []
            Lx = []
            if b + 1 < NB:
                if sg == 2:
                    for st in range(8):
                        buf, key, chn = xa[st % 2], ("xa", st % 2), ("QXA", st % 2)
                        Lx.extend(stats_units(nxt, st, buf, key, chn))
                if sg == 3:
                    for st in range(8, 16):
                        buf, key, chn = xa[st % 2], ("xa", st % 2), ("QXA", st % 2)
                        Lx.extend(stats_units(nxt, st, buf, key, chn))
                    Lx.append((0, lambda: emit_colmath(nxt, 0, 16)))
            Lw = wrest if (b == 0 and sg == 1) else []
            Lp = []
            prep_tps = []
            if sg <= 3:
                for s_ in range(4):
                    st = 4 * sg + s_
                    lc, tp = prep_units2(b, st, xr[st % 2], ("xr", st % 2),
                                         load_chan=("QXR", st % 2), load_eng="G")
                    Lp.extend(lc)
                    prep_tps.extend(tp)
            ph1_tail = []
            if b + 1 < NB and sg == 4:
                per_st = []
                for st in range(12):
                    lc, tp = ph1_units2(nxt, st, xa[st % 2], ("xa", st % 2),
                                        load_chan=("QXA", st % 2), eng="V",
                                        tok=(xntok[st % 4], ("xntok", st % 4)))
                    per_st.append((lc, tp))
                for st in range(4):
                    Lp.extend(per_st[st][0])
                for st in range(12):
                    ph1_tail.extend(per_st[st][1])
                    if st + 4 < 12:
                        ph1_tail.extend(per_st[st + 4][0])
            body = mix(Lsim, Lffn, Lfin, Lx, Lw, Lp)
            play(body)
            play(prep_tps)
            play(ph1_tail)
            if b + 1 < NB and sg == 5:
                play([crow_memset_unit(nxt)])
                secA = []
                ph1_lc3, ph1_tp3 = [], []
                for st in range(12, 16):
                    lc, tp = ph1_units2(nxt, st, xa[st % 2], ("xa", st % 2),
                                        load_chan=("QXA", st % 2), eng="V",
                                        tok=(xntok[st % 4], ("xntok", st % 4)))
                    ph1_lc3.extend(lc)
                    ph1_tp3.extend(tp)
                for kc in range(3):
                    for q in range(4):
                        secA.append(sim_block_unit(nxt, q, kc, kc, 4, False))
                        if kc > 0:
                            secA.append(cs_unit(nxt, q, kc))
                fin3 = fin_units(b, 3)
                play(mix(fin3, secA, ph1_lc3))
                sec_b = list(ph1_tp3)
                p_lc, p_tp = [], []
                for s_ in range(4):
                    lc, tp = prep_units2(nxt, s_, xr[s_ % 2], ("xr", s_ % 2),
                                         load_chan=("QXR", s_ % 2), load_eng="G")
                    p_lc.extend(lc)
                    p_tp.extend(tp)
                sec_b.extend(p_lc)
                for q in range(4):
                    sec_b.append(sim_block_unit(nxt, q, 3, 3, 4, True))
                    sec_b.append(cs_unit(nxt, q, 3))
                sec_b.append(crow_unit(nxt, 0))
                sec_b.extend(p_tp)
                play(sec_b)
            if b + 1 >= NB and sg == 5:
                play(fin_units(b, 3))

    for b in range(NB):
        batch_steps(b)

    sch.finalize()
    globals()["LAST_SCHED"] = sch
    with nc.Block() as block:
        @block.tensor
        def _(eng):
            sch.emit("T", eng, sems)

        @block.vector
        def _(eng):
            sch.emit("V", eng, sems)

        @block.scalar
        def _(eng):
            sch.emit("S", eng, sems)

        @block.gpsimd
        def _(eng):
            sch.emit("G", eng, sems)
            for k in range(3):
                n_store = sch.chan_count.get(("QST", k), 0)
                if n_store:
                    eng.wait_ge(sems[("QST", k)], 16 * n_store)

        @block.sync
        def _(eng):
            sch.emit("SY", eng, sems)
    es.close()
    return nc


def kernel(L0_attn_data, L0_attn_W1, L0_attn_b1, L0_attn_W2,
           L0_ffn_U1, L0_ffn_V1, L0_ffn_b1, L0_ffn_U2, L0_ffn_V2, L0_ffn_b2):
    n = 8
    def h(a):
        a = np.asarray(a)
        assert a.dtype == np.float32, a.dtype
        return a
    xfull = np.ascontiguousarray(h(L0_attn_data))
    com = {
        "W1": np.ascontiguousarray(h(L0_attn_W1).reshape(8, 128).T),
        "b1": np.ascontiguousarray(h(L0_attn_b1).reshape(8, 128).T),
        "W2": np.ascontiguousarray(h(L0_attn_W2)),
        "U1": np.ascontiguousarray(h(L0_ffn_U1)),
        "V1": np.ascontiguousarray(h(L0_ffn_V1)),
        "fb1": np.ascontiguousarray(h(L0_ffn_b1).reshape(32, 128).T),
        "U2": np.ascontiguousarray(h(L0_ffn_U2)),
        "V2": np.ascontiguousarray(h(L0_ffn_V2)),
        "b2": np.ascontiguousarray(h(L0_ffn_b2).reshape(1, 1024)),
        "idb": np.eye(128, dtype=np.float32).astype(ml_dtypes.bfloat16),
        "idf": np.eye(128, dtype=np.float32),
    }
    nc = build_program()
    in_maps = [dict(com, x=np.ascontiguousarray(xfull[2 * c:2 * c + 2])) for c in range(n)]
    res = run_bass_kernel_spmd(nc, in_maps, core_ids=list(range(n)))
    return np.concatenate([res.results[c]["out"] for c in range(n)], axis=0)


# version 27
# speedup vs baseline: 1.1305x; 1.1305x over previous
import numpy as np
import ml_dtypes
import concourse.bass as bass
import concourse.mybir as mybir
from concourse.bass_utils import run_bass_kernel_spmd
from concourse.alu_op_type import AluOpType

F32 = mybir.dt.float32
BF16 = mybir.dt.bfloat16
AFT = mybir.ActivationFunctionType
AX = mybir.AxisListType

NB = 2
POOL_DEPTH = 3
PH1_ENG = "V"
SUM_ENG = "S"
QLAG = 2
HBF_N = 4
FIN_ADD_ENG = "V"
CS_ADD_ENG = "V"
PQ_EVAC_ENG = "V"
CF_POS = 20
STEADY_EVAC = "S"
SEQ = 2048
D = 1024
DC = 8
NT = 4
NQ = 16

class _Op:
    __slots__ = ("stream", "chan", "fn", "dma", "pos", "chan_idx", "marked",
                 "waits", "drain", "clock", "tag")

class Sched:
    def __init__(self, streams, total_chans=()):
        self.streams = {s: [] for s in streams}
        self.total_chans = set(total_chans)
        self.chan_count = {}
        self.res = {}
        self.stream_clock = {s: {} for s in streams}
        self.last_drain = {s: -1 for s in streams}

    def add(self, stream, fn, reads=(), writes=(), chan=None, dma=False, tag=""):
        chan = chan or stream
        op = _Op()
        op.stream, op.chan, op.fn, op.dma, op.tag = stream, chan, fn, dma, tag
        op.pos = len(self.streams[stream])
        op.chan_idx = self.chan_count.get(chan, 0)
        self.chan_count[chan] = op.chan_idx + 1
        op.marked = dma
        op.waits = {}
        op.drain = False
        deps = []
        for k in reads:
            st = self.res.get(k)
            if st and st[0] is not None:
                deps.append((st[0], False))
        for k in writes:
            st = self.res.get(k)
            if st:
                if st[0] is not None:
                    deps.append((st[0], False))
                for r in st[1].values():
                    deps.append((r, True))
        clock = dict(self.stream_clock[stream])
        for dep, is_war in deps:
            if dep is op:
                continue
            if dep.dma:
                if dep.chan in self.total_chans:
                    idx = self.chan_count[dep.chan] - 1
                else:
                    idx = dep.chan_idx
                if clock.get(dep.chan, -1) < idx:
                    op.waits[dep.chan] = max(op.waits.get(dep.chan, -1), idx)
                    for c2, i2 in dep.clock.items():
                        if clock.get(c2, -1) < i2:
                            clock[c2] = i2
                    clock[dep.chan] = idx
            elif dep.stream == stream:
                if stream != "T":
                    if clock.get(dep.chan, -1) < dep.chan_idx:
                        dep.marked = True
                        op.waits[dep.chan] = max(op.waits.get(dep.chan, -1), dep.chan_idx)
                        clock[dep.chan] = dep.chan_idx
            else:
                if clock.get(dep.chan, -1) < dep.chan_idx:
                    dep.marked = True
                    op.waits[dep.chan] = max(op.waits.get(dep.chan, -1), dep.chan_idx)
                    for c2, i2 in dep.clock.items():
                        if clock.get(c2, -1) < i2:
                            clock[c2] = i2
                    clock[dep.chan] = dep.chan_idx
        op.clock = clock
        self.stream_clock[stream] = clock
        for k in reads:
            self.res.setdefault(k, [None, {}])[1][stream] = op
        for k in writes:
            self.res[k] = [op, {}]
        self.streams[stream].append(op)
        return op

    def finalize(self):
        self.mark_val = {}
        per_chan_ops = {}
        for s, ops in self.streams.items():
            for o in ops:
                per_chan_ops.setdefault(o.chan, []).append(o)
        for c, ops in per_chan_ops.items():
            ops.sort(key=lambda o: o.chan_idx)
            acc = 0
            vals = []
            for o in ops:
                if o.marked:
                    acc += 16 if o.dma else 1
                vals.append(acc)
            self.mark_val[c] = vals

    def wait_value(self, chan, idx):
        return self.mark_val[chan][idx]

    def emit(self, stream, eng, sems):
        for o in self.streams[stream]:
            if o.drain:
                eng.drain()
            for c, idx in sorted(o.waits.items(), key=lambda kv: str(kv[0])):
                v = self.wait_value(c, idx)
                if v > 0:
                    eng.wait_ge(sems[c], v)
            inst = o.fn(eng)
            if o.marked:
                if inst is None:
                    inst = eng.nop()
                inst.then_inc(sems[o.chan], 16 if o.dma else 1)


DO_MIX = True

def mix(*lists):
    out = []
    if not DO_MIX:
        for L in lists:
            out.extend(L)
        return out
    tot = [max(sum(c for c, _ in L), 1) for L in lists]
    done = [0.0] * len(lists)
    idx = [0] * len(lists)
    while True:
        best, bi = None, -1
        for i, L in enumerate(lists):
            if idx[i] < len(L):
                f = done[i] / tot[i]
                if best is None or f < best:
                    best, bi = f, i
        if bi < 0:
            break
        c, fn = lists[bi][idx[bi]]
        idx[bi] += 1
        done[bi] += c
        out.append((c, fn))
    return out

def play(units):
    for _, fn in units:
        fn()


def build_program():
    nc = bass.Bass("TRN2", target_bir_lowering=False)
    from contextlib import ExitStack
    es = ExitStack()

    x = nc.dram_tensor("x", [NB, SEQ, D], F32, kind="ExternalInput")
    W1 = nc.dram_tensor("W1", [128, 8], F32, kind="ExternalInput")
    b1 = nc.dram_tensor("b1", [128, 8], F32, kind="ExternalInput")
    W2 = nc.dram_tensor("W2", [2 * D, D], F32, kind="ExternalInput")
    U1 = nc.dram_tensor("U1", [D, 256], F32, kind="ExternalInput")
    V1 = nc.dram_tensor("V1", [256, 4096], F32, kind="ExternalInput")
    fb1 = nc.dram_tensor("fb1", [128, 32], F32, kind="ExternalInput")
    U2 = nc.dram_tensor("U2", [4096, 256], F32, kind="ExternalInput")
    V2 = nc.dram_tensor("V2", [256, D], F32, kind="ExternalInput")
    b2 = nc.dram_tensor("b2", [1, 1024], F32, kind="ExternalInput")
    idb = nc.dram_tensor("idb", [128, 128], BF16, kind="ExternalInput")
    idf = nc.dram_tensor("idf", [128, 128], F32, kind="ExternalInput")
    out = nc.dram_tensor("out", [NB, SEQ, D], F32, kind="ExternalOutput")

    def sb(name, shape, dt=F32):
        return es.enter_context(nc.sbuf_tensor(name, shape, dt))

    def ps(name, shape, dt=F32):
        return es.enter_context(nc.psum_tensor(name, shape, dt))

    w2sb = sb("w2sb", [128, 16 * 1024], BF16)
    u1sb = sb("u1sb", [128, 8 * 256], BF16)
    v1sb = sb("v1sb", [128, 2 * 4096], BF16)
    u2sb = sb("u2sb", [128, 32 * 256], BF16)
    v2sb = sb("v2sb", [128, 2 * 1024], BF16)
    wstage = [sb(f"wstage{i}", [128, 2048]) for i in range(2)]
    w1c = sb("w1c", [128, 8])
    b1c = sb("b1c", [128, 8])
    fb1c = sb("fb1c", [128, 32])
    b2bf = sb("b2bf", [1, 1024], BF16)
    onesf = sb("onesf", [1, 128])
    onesb = sb("onesb", [128, 1], BF16)
    onesbr = sb("onesbr", [1, 128], BF16)
    identbf = sb("identbf", [128, 128], BF16)
    identf = sb("identf", [128, 128])

    nfm = sb("nfm", [128, 8 * 2048], BF16)
    xa = [sb(f"xa{i}", [128, 1024]) for i in range(2)]
    xr = [sb(f"xr{i}", [128, 1024]) for i in range(2)]
    outsb = [sb(f"outsb{i}", [128, 1024]) for i in range(2)]
    xnfm = [sb(f"xnfm{i}", [128, 8 * 512], BF16) for i in range(2)]
    ntok = [sb(f"ntok{i}", [128, 1024], BF16) for i in range(2)]
    xntok = [sb(f"xntok{i}", [128, 1024], BF16) for i in range(4)]
    pfm = [sb(f"pfm{i}", [128, 2 * 512], BF16) for i in range(2)]
    qfm = [sb(f"qfm{i}", [128, 2 * 512], BF16) for i in range(2)]
    hbf = [sb(f"hbf{i}", [128, 512], BF16) for i in range(HBF_N)]
    e512 = [wstage[1][:, 0:512], wstage[1][:, 512:1024]]
    vjunk = sb("vjunk", [128, 2048], BF16)
    cpart = sb("cpart16", [128, 16])
    call = sb("call", [128, 16])
    crow4 = [sb(f"crow4{i}", [4, 128]) for i in range(2)]
    crow1 = sb("crow1", [1, 2048])
    cscr = sb("cscr", [1, 512])
    cols = {}
    for n in ["sum", "ssq", "mean", "var", "u", "ru", "sq", "rs", "rs2",
              "scaleA", "biasA", "invs", "biasB", "t1", "t2"]:
        cols[n] = [sb(f"c_{n}{p}", [128, 16]) for p in range(2)]
    sq32 = [sb(f"sq32_{p}", [128, 32]) for p in range(2)]
    iv32 = [sb(f"iv32_{p}", [128, 32]) for p in range(2)]

    s0 = ps("s0", [128, 512]); s1 = ps("s1", [128, 512])
    s2 = ps("s2", [128, 512])
    s3 = ps("s3", [128, 512]); hx = ps("hx", [128, 512])
    q0 = ps("q0", [128, 512]); q1 = ps("q1", [128, 512])
    tb = ps("tb0", [128, 1024], BF16)
    POOL_N = POOL_DEPTH
    pool = [s0, s1, s2, s3, hx]

    streams = ["T", "V", "S", "G", "SY"]
    dma_chans = (["QC"] + [("QW", i) for i in range(2)] + [("QXA", i) for i in range(2)]
                 + [("QXR", i) for i in range(2)] + [("QXO", i) for i in range(2)]
                 + [("QXS", i) for i in range(2)]
                 + [("QXF", i) for i in range(2)] + [("QTB", 0)]
                 + [("QTN", i) for i in range(2)]
                 + [("QST", i) for i in range(3)])
    sch = Sched(streams, total_chans=["QC"])
    sems = {}
    for c in streams + dma_chans:
        nm = c if isinstance(c, str) else c[0] + str(c[1])
        sems[c] = es.enter_context(nc.semaphore("sem_" + nm))

    A = sch.add
    state = {"pool": 0, "vj": 0}

    def pool_slot():
        i = state["pool"] % POOL_N
        state["pool"] += 1
        return pool[i], ("ps", i)

    def h_bank(fc):
        nh = 5 - POOL_N
        i = fc % nh
        return pool[POOL_N + i], ("ps", POOL_N + i)

    def vj_slot():
        i = state["vj"] % 4
        state["vj"] += 1
        return i

    A("SY", lambda e: e.dma_start(out=identbf[:, :], in_=idb[:, :]),
      writes=[("identbf",)], chan="QC", dma=True)
    A("SY", lambda e: e.dma_start(out=identf[:, :], in_=idf[:, :]),
      writes=[("identf",)], chan="QC", dma=True)
    A("SY", lambda e: e.dma_start(out=w1c[:, :], in_=W1[:, :]),
      writes=[("w1c",)], chan="QC", dma=True)
    A("SY", lambda e: e.dma_start(out=b1c[:, :], in_=b1[:, :]),
      writes=[("b1c",)], chan="QC", dma=True)
    A("SY", lambda e: e.dma_start(out=fb1c[:, :], in_=fb1[:, :]),
      writes=[("fb1c",)], chan="QC", dma=True)
    A("SY", lambda e: e.dma_start(out=crow1[0:1, 0:1024], in_=b2[0:1, :]),
      writes=[("crow1", 0), ("crow1", 1)], chan="QC", dma=True)
    A("V", lambda e: e.tensor_copy(b2bf[0:1, :], crow1[0:1, 0:1024]),
      reads=[("crow1", 0), ("crow1", 1)], writes=[("b2bf",)])
    A("V", lambda e: e.memset(onesf[:, :], 1.0), writes=[("onesf",)])
    A("V", lambda e: e.memset(onesb[:, :], 1.0), writes=[("onesb",)])
    A("V", lambda e: e.memset(onesbr[:, :], 1.0), writes=[("onesbr",)])

    def weight_units():
        chunks = []
        u1v = U1.rearrange("(c p) r -> p c r", p=128)
        chunks.append((lambda: u1v[:, :, :], u1sb, 0, 2048, "u1", "S"))
        v1v = V1.rearrange("(c p) f -> p c f", p=128)
        for j in range(4):
            c, hh = j // 2, j % 2
            chunks.append((lambda c=c, hh=hh: v1v[:, c:c + 1, hh * 2048:(hh + 1) * 2048],
                           v1sb, c * 4096 + hh * 2048, 2048, "v1", "S"))
        u2v = U2.rearrange("(c p) r -> p c r", p=128)
        for j in range(4):
            chunks.append((lambda j=j: u2v[:, 8 * j:8 * (j + 1), :], u2sb,
                           j * 2048, 2048, "u2", "S"))
        v2v = V2.rearrange("(c p) d -> p c d", p=128)
        chunks.append((lambda: v2v[:, :, :], v2sb, 0, 2048, "v2", "S"))
        w2v = W2.rearrange("(c p) d -> p c d", p=128)
        for j in range(8):
            chunks.append((lambda j=j: w2v[:, 2 * j:2 * (j + 1), :], w2sb,
                           j * 2048, 2048, "w2", "S"))
        units = []
        for i, (apf, dest, off, n, wname, eng) in enumerate(chunks):
            stg = wstage[i % 2]
            eng = "S" if i % 2 == 0 else "V"
            def emit(apf=apf, stg=stg, i=i, dest=dest, off=off, n=n, wname=wname, eng=eng):
                A("SY", lambda e: e.dma_start(out=stg[:, :], in_=apf()),
                  writes=[("wstage", i % 2)], chan=("QW", i % 2), dma=True)
                if eng == "S":
                    A("S", lambda e: e.copy(dest[:, off:off + n], stg[:, :]),
                      reads=[("wstage", i % 2)], writes=[("w", wname)])
                else:
                    A("V", lambda e: e.tensor_copy(dest[:, off:off + n], stg[:, :]),
                      reads=[("wstage", i % 2)], writes=[("w", wname)])
            units.append((4000, emit))
        return units

    def emit_colmath(b, c0, c1):
        p = b % 2
        C = {n: cols[n][p][:, c0:c1] for n in cols}
        K = lambda n: ("cm", p, n, c0)
        rd_sum = [("c_sum", p, st) for st in range(c0, c1)]
        rd_ssq = [("c_ssq", p, st) for st in range(c0, c1)]
        def vop(fn, reads, writes):
            A("V", fn, reads=reads, writes=writes)
        vop(lambda e: e.tensor_scalar(out=C["mean"], in0=C["sum"],
            scalar1=1.0 / D, scalar2=None, op0=AluOpType.mult), rd_sum, [K("mean")])
        vop(lambda e: e.tensor_scalar(out=C["t1"], in0=C["ssq"],
            scalar1=1.0 / D, scalar2=None, op0=AluOpType.mult), rd_ssq, [K("t1")])
        vop(lambda e: e.tensor_tensor(out=C["t2"], in0=C["mean"],
            in1=C["mean"], op=AluOpType.mult), [K("mean")], [K("t2")])
        vop(lambda e: e.tensor_tensor(out=C["var"], in0=C["t1"],
            in1=C["t2"], op=AluOpType.subtract), [K("t1"), K("t2")], [K("var")])
        vop(lambda e: e.tensor_scalar(out=C["u"], in0=C["var"],
            scalar1=1e-5, scalar2=None, op0=AluOpType.add), [K("var")], [K("u")])
        vop(lambda e: e.reciprocal(C["ru"], C["u"]), [K("u")], [K("ru")])
        vop(lambda e: e.tensor_tensor(out=C["t1"], in0=C["var"],
            in1=C["ru"], op=AluOpType.mult), [K("var"), K("ru")], [K("t1")])
        vop(lambda e: e.tensor_scalar(out=C["sq"], in0=C["t1"],
            scalar1=float(D), scalar2=1e-12, op0=AluOpType.mult, op1=AluOpType.max),
            [K("t1")], [K("sq")])
        vop(lambda e: e.reciprocal(C["rs"], C["sq"]), [K("sq")], [K("rs")])
        sqa = sq32[p][:, c0:c1]; sqb = sq32[p][:, 16 + c0:16 + c1]
        iva = iv32[p][:, c0:c1]; ivb = iv32[p][:, 16 + c0:16 + c1]
        vop(lambda e: e.tensor_copy(sqa, C["ru"]), [K("ru")], [K("s32a")])
        vop(lambda e: e.tensor_copy(sqb, C["rs"]), [K("rs")], [K("s32b")])
        A("S", lambda e: e.activation(out=iva, in_=sqa, func=AFT.Sqrt),
          reads=[K("s32a")], writes=[K("iva")])
        A("S", lambda e: e.activation(out=ivb, in_=sqb, func=AFT.Sqrt),
          reads=[K("s32b")], writes=[K("ivb")])
        def newton(ycols, ykey, acol, akey, t1, outc, outk):
            vop(lambda e: e.tensor_tensor(out=t1, in0=ycols, in1=ycols,
                op=AluOpType.mult), [ykey], [K(outk + "t1")])
            vop(lambda e: e.tensor_tensor(out=t1, in0=t1, in1=acol,
                op=AluOpType.mult), [K(outk + "t1"), akey], [K(outk + "t1")])
            vop(lambda e: e.tensor_scalar(out=t1, in0=t1, scalar1=-0.5,
                scalar2=1.5, op0=AluOpType.mult, op1=AluOpType.add),
                [K(outk + "t1")], [K(outk + "t1")])
            vop(lambda e: e.tensor_tensor(out=outc, in0=ycols, in1=t1,
                op=AluOpType.mult), [ykey, K(outk + "t1")], [K(outk)])
        newton(iva, K("iva"), C["u"], K("u"), C["t1"], C["invs"], "invs")
        newton(ivb, K("ivb"), C["sq"], K("sq"), C["t2"], C["rs2"], "rs2")
        vop(lambda e: e.tensor_tensor(out=C["scaleA"], in0=C["invs"],
            in1=C["rs2"], op=AluOpType.mult),
            [K("invs"), K("rs2")], [K("scaleA")])
        vop(lambda e: e.scalar_tensor_tensor(out=C["biasA"], in0=C["mean"],
            scalar=-1.0, in1=C["scaleA"], op0=AluOpType.mult, op1=AluOpType.mult),
            [K("mean"), K("scaleA")], [K("biasA")] + [("cmA", p, st) for st in range(c0, c1)])
        vop(lambda e: e.scalar_tensor_tensor(out=C["biasB"], in0=C["mean"],
            scalar=-1.0, in1=C["invs"], op0=AluOpType.mult, op1=AluOpType.mult),
            [K("mean"), K("invs")], [K("biasB")] + [("cmB", p, st) for st in range(c0, c1)])

    def tp_group(src, skey, dst, dkey, st, evac=None):
        def emit():
            for dc in range(DC):
                A("T", lambda e, dc=dc:
                  e.transpose(tb[:, dc * 128:(dc + 1) * 128],
                              src[:, dc * 128:(dc + 1) * 128], identbf[:, :]),
                  reads=[skey, ("identbf",)], writes=[("ptb",)])
            eng = evac or STEADY_EVAC
            if eng == "V":
                A("V", lambda e:
                  e.tensor_copy(dst.rearrange("p (c f) -> p c f", c=8)[:, :,
                                                                      st * 128:(st + 1) * 128],
                                tb[:, :].rearrange("p (c f) -> p c f", c=8)),
                  reads=[("ptb",)], writes=[dkey])
            else:
                A("S", lambda e:
                  e.copy(dst.rearrange("p (c f) -> p c f", c=8)[:, :,
                                                               st * 128:(st + 1) * 128],
                         tb[:, :].rearrange("p (c f) -> p c f", c=8)),
                  reads=[("ptb",)], writes=[dkey])
        return (1600, emit)

    def ph1_units2(b, st, slot_buf, slot_key, load_chan=None, eng="V", tok=None, evac=None):
        p = b % 2
        C = cols
        tbuf, tkey = tok if tok is not None else (ntok[st % 2], ("ntok", st % 2))
        lc = []
        if load_chan is not None:
            def ld():
                A("SY", lambda e: e.dma_start(out=slot_buf[:, :],
                                              in_=x[b, st * 128:(st + 1) * 128, :]),
                  writes=[slot_key], chan=load_chan, dma=True)
            lc.append((600, ld))
        def cast():
            if eng == "V":
                A("V", lambda e: e.tensor_scalar(out=tbuf[:, :], in0=slot_buf[:, :],
                                scalar1=C["scaleA"][p][:, st:st + 1],
                                scalar2=C["biasA"][p][:, st:st + 1],
                                op0=AluOpType.mult, op1=AluOpType.add),
                  reads=[slot_key, ("cmA", p, st)], writes=[tkey])
            else:
                A("S", lambda e: e.activation(out=tbuf[:, :], in_=slot_buf[:, :],
                                func=AFT.Identity, scale=C["scaleA"][p][:, st:st + 1],
                                bias=C["biasA"][p][:, st:st + 1]),
                  reads=[slot_key, ("cmA", p, st)], writes=[tkey])
        lc.append((600, cast))
        t = st // 4
        return lc, [tp_group(tbuf, tkey, nfm, ("nfm", t), st, evac=evac)]

    def ph1_units(b, st, slot_buf, slot_key, load_chan=None, eng="V", tok=None):
        lc, tp = ph1_units2(b, st, slot_buf, slot_key, load_chan, eng, tok)
        return lc + tp

    def prep_units2(b, st, slot_buf, slot_key, load_chan=None, load_eng="G"):
        p = b % 2
        C = cols
        t = st // 4
        lc = []
        if load_chan is not None:
            def ld():
                A(load_eng, lambda e: e.dma_start(out=slot_buf[:, :],
                                                  in_=x[b, st * 128:(st + 1) * 128, :]),
                  writes=[slot_key], chan=load_chan, dma=True)
            lc.append((600, ld))
        def cast():
            A("V", lambda e: e.tensor_scalar(out=xntok[st % 4][:, :], in0=slot_buf[:, :],
                            scalar1=C["invs"][p][:, st:st + 1],
                            scalar2=C["biasB"][p][:, st:st + 1],
                            op0=AluOpType.mult, op1=AluOpType.add),
              reads=[slot_key, ("cmB", p, st)], writes=[("xntok", st % 4)])
        lc.append((600, cast))
        return lc, [tp_group(xntok[st % 4], ("xntok", st % 4), xnfm[t % 2],
                             ("xnfm", t % 2), st % 4)]

    def prep_units(b, st, slot_buf, slot_key, load_chan=None, load_eng="G"):
        lc, tp = prep_units2(b, st, slot_buf, slot_key, load_chan, load_eng)
        return lc + tp

    def stats_units(b, st, slot_buf, slot_key, load_chan, sum_eng="V"):
        p = b % 2
        units = []
        def ld():
            A("SY", lambda e: e.dma_start(out=slot_buf[:, :],
                                          in_=x[b, st * 128:(st + 1) * 128, :]),
              writes=[slot_key], chan=load_chan, dma=True)
        def do_sum():
            if sum_eng == "V":
                A("V", lambda e: e.tensor_scalar(out=ntok[0][:, :], in0=slot_buf[:, :],
                                scalar1=0.0, scalar2=None, op0=AluOpType.add,
                                op1=AluOpType.add,
                                accum_out=cols["sum"][p][:, st:st + 1]),
                  reads=[slot_key], writes=[("c_sum", p, st), ("ntok", 0)])
            else:
                A("S", lambda e: e.activation(out=ntok[0][:, :], in_=slot_buf[:, :],
                                func=AFT.Identity,
                                accum_out=cols["sum"][p][:, st:st + 1]),
                  reads=[slot_key], writes=[("c_sum", p, st), ("ntok", 0)])
        def do_ssq():
            A("S", lambda e: e.activation(out=ntok[1][:, :], in_=slot_buf[:, :],
                            func=AFT.Square,
                            accum_out=cols["ssq"][p][:, st:st + 1]),
              reads=[slot_key], writes=[("c_ssq", p, st), ("ntok", 1)])
        units.append((2500, ld))
        units.append((1200, do_sum))
        units.append((1200, do_ssq))
        return units

    def sim_block_unit(b, q, kc, j, nb_, last):
        def emit():
            bank, bkey = pool_slot()
            vs = vj_slot()
            state["blkvs", b, q, kc] = vs
            for dc in range(DC):
                A("T", lambda e, dc=dc:
                  e.matmul(bank[:, :],
                           nfm[:, dc * 2048 + q * 128: dc * 2048 + (q + 1) * 128],
                           nfm[:, dc * 2048 + kc * 512: dc * 2048 + (kc + 1) * 512],
                           start=(dc == 0), stop=(dc == DC - 1)),
                  reads=[("nfm", q // 4), ("nfm", kc)],
                  writes=[bkey], tag=f"sim{b}_{q}_{kc}")
            col = 4 * (q % 4) + j
            A("V", lambda e:
              e.tensor_scalar(out=vjunk[:, vs * 512:(vs + 1) * 512],
                              in0=bank[:, :], scalar1=0.0,
                              scalar2=None, op0=AluOpType.max, op1=AluOpType.add,
                              accum_out=cpart[:, col:col + 1]),
              reads=[bkey], writes=[("cp", col), ("vj", vs)])
            if last:
                A("V", lambda e:
                  e.reduce_sum(call[:, q:q + 1], cpart[:, 4 * (q % 4):4 * (q % 4) + nb_],
                               axis=AX.X),
                  reads=[("cp", 4 * (q % 4) + jj) for jj in range(nb_)],
                  writes=[("call", q)])
        return (4400, emit)

    def cs_unit(b, q, kc):
        def emit():
            vs = state["blkvs", b, q, kc]
            bank, bkey = pool_slot()
            A("T", lambda e:
              e.matmul(bank[0:1, 0:512], onesb[:, :], vjunk[:, vs * 512:(vs + 1) * 512],
                       start=True, stop=True),
              reads=[("vj", vs), ("onesb",)], writes=[bkey], tag=f"cs{b}_{q}_{kc}")
            A(CS_ADD_ENG, lambda e:
              e.tensor_tensor(out=crow1[0:1, kc * 512:(kc + 1) * 512],
                              in0=crow1[0:1, kc * 512:(kc + 1) * 512],
                              in1=bank[0:1, 0:512], op=AluOpType.add),
              reads=[bkey, ("crow1", kc)], writes=[("crow1", kc)])
        return (700, emit)

    def sim_tile_units(b, t):
        units = []
        pend = []
        if t == 0:
            blocks = [(q, kc, kc, 4, kc == 3) for kc in range(4) for q in range(4)]
        else:
            blocks = [(q, kc, kc - t, 4 - t, kc == 3)
                      for q in range(4 * t, 4 * t + 4) for kc in range(t, 4)]
        for (q, kc, j, nb_, last) in blocks:
            units.append(sim_block_unit(b, q, kc, j, nb_, last))
            if len(pend) >= 2:
                units.append(pend.pop(0))
            if kc > q // 4:
                pend.append(cs_unit(b, q, kc))
        units.extend(pend)
        units.append(crow_unit(b, t))
        return units

    def crow_unit(b, t):
        def emit():
            bank, bkey = pool_slot()
            A("T", lambda e: e.transpose(bank[0:4, 0:128], call[:, 4 * t:4 * t + 4],
                                         identf[:, :]),
              reads=[("call", 4 * t + i) for i in range(4)] + [("identf",)],
              writes=[bkey])
            A("V", lambda e: e.tensor_copy(crow4[t % 2][:, :], bank[0:4, 0:128]),
              reads=[bkey], writes=[("crow4", t % 2)])
            for qi in range(4):
                A("G", lambda e, qi=qi:
                  e.dma_start(out=cscr[0:1, qi * 128:(qi + 1) * 128],
                              in_=crow4[t % 2][qi:qi + 1, :]),
                  reads=[("crow4", t % 2)], writes=[("cscr",)],
                  chan=("QTN", t % 2), dma=True)
            A("V", lambda e:
              e.tensor_tensor(out=crow1[0:1, t * 512:(t + 1) * 512],
                              in0=crow1[0:1, t * 512:(t + 1) * 512],
                              in1=cscr[0:1, 0:512], op=AluOpType.add),
              reads=[("cscr",), ("crow1", t)], writes=[("crow1", t)])
        return (300, emit)

    def cf_unit(b, t):
        def emit():
            bank, bkey = pool_slot()
            A("T", lambda e: e.matmul(bank[:, :], onesf[:, :],
                                      crow1[0:1, t * 512:(t + 1) * 512],
                                      start=True, stop=True),
              reads=[("onesf",), ("crow1", t)], writes=[bkey], tag=f"T2_{b}_{t}")
            for dc in range(DC):
                A("S", lambda e, dc=dc:
                  e.activation(out=e512[dc % 2][:, :], in_=bank[:, :], func=AFT.Exp,
                               scale=w1c[:, dc:dc + 1], bias=b1c[:, dc:dc + 1]),
                  reads=[bkey, ("w1c",), ("b1c",)],
                  writes=[("e512", dc % 2), ("wstage", 1)])
                A("S", lambda e, dc=dc:
                  e.activation(out=nfm[:, dc * 2048 + t * 512: dc * 2048 + (t + 1) * 512],
                               in_=e512[dc % 2][:, :], func=AFT.Ln, bias=1.0, scale=1.0),
                  reads=[("e512", dc % 2), ("wstage", 1)], writes=[("nfm", t)],
                  tag=f"cf{b}_{t}_{dc}")
        return (2600, emit)

    def crow_memset_unit(b):
        def emit():
            A("V", lambda e: e.memset(crow1[:, :], 0.0),
              writes=[("crow1", tt) for tt in range(NT)])
        return (0, emit)

    def ffn_units(b, t):
        units = []
        def P(rc):
            def emit():
                bank, bkey = h_bank(rc)
                for dc in range(DC):
                    A("T", lambda e, dc=dc:
                      e.matmul(bank[:, :],
                               u1sb[:, dc * 256 + rc * 128: dc * 256 + (rc + 1) * 128],
                               xnfm[t % 2][:, dc * 512:(dc + 1) * 512],
                               start=(dc == 0), stop=(dc == DC - 1)),
                      reads=[("xnfm", t % 2), ("w", "u1")],
                      writes=[bkey], tag=f"P{b}_{t}")
                if PQ_EVAC_ENG == "V":
                    A("V", lambda e: e.tensor_copy(pfm[t % 2][:, rc * 512:(rc + 1) * 512],
                                                   bank[:, :]),
                      reads=[bkey], writes=[("pfm", t % 2)])
                else:
                    A("S", lambda e: e.copy(pfm[t % 2][:, rc * 512:(rc + 1) * 512],
                                            bank[:, :]),
                      reads=[bkey], writes=[("pfm", t % 2)])
            return (4400, emit)
        units.append(P(0))
        units.append(P(1))
        def H(fc):
            bank, bkey = h_bank(fc)
            for c in range(2):
                A("T", lambda e, c=c:
                  e.matmul(bank[:, :], v1sb[:, c * 4096 + fc * 128:c * 4096 + (fc + 1) * 128],
                           pfm[t % 2][:, c * 512:(c + 1) * 512],
                           start=(c == 0), stop=(c == 1)),
                  reads=[("pfm", t % 2), ("w", "v1")],
                  writes=[bkey], tag=f"H{b}_{t}_{fc}")
            A("S", lambda e:
              e.activation(out=hbf[fc % HBF_N][:, :], in_=bank[:, :], func=AFT.Gelu_apprx_tanh,
                           scale=1.0, bias=fb1c[:, fc:fc + 1]),
              reads=[bkey, ("fb1c",)], writes=[("hbf", fc % HBF_N)])
        def Q(fc):
            for rc in range(2):
                A("T", lambda e, rc=rc:
                  e.matmul([q0, q1][rc][:, :],
                           u2sb[:, fc * 256 + rc * 128: fc * 256 + (rc + 1) * 128],
                           hbf[fc % HBF_N][:, :], start=(fc == 0), stop=(fc == 31)),
                  reads=[("hbf", fc % HBF_N), ("w", "u2")],
                  writes=[("pq0",) if rc == 0 else ("pq1",)], tag=f"Q{b}_{t}")
        def HQ(fc):
            def emit():
                H(fc)
                if fc >= QLAG:
                    Q(fc - QLAG)
            return (2300, emit)
        for fc in range(32):
            units.append(HQ(fc))
        def Qlast():
            for fc in range(32 - QLAG, 32):
                Q(fc)
            for rc in range(2):
                if PQ_EVAC_ENG == "V":
                    A("V", lambda e, rc=rc:
                      e.tensor_copy(qfm[t % 2][:, rc * 512:(rc + 1) * 512], [q0, q1][rc][:, :]),
                      reads=[("pq0",) if rc == 0 else ("pq1",)], writes=[("qfm", t % 2)])
                else:
                    A("S", lambda e, rc=rc:
                      e.copy(qfm[t % 2][:, rc * 512:(rc + 1) * 512], [q0, q1][rc][:, :]),
                      reads=[("pq0",) if rc == 0 else ("pq1",)], writes=[("qfm", t % 2)])
        units.append((1800, Qlast))
        return units

    def fin_units(b, t):
        units = []
        for s_ in range(4):
            st = 4 * t + s_
            def gx(st=st):
                A("G", lambda e: e.dma_start(out=outsb[st % 2][:, :],
                                             in_=x[b, st * 128:(st + 1) * 128, :]),
                  writes=[("outsb", st % 2)], chan=("QXF", st % 2), dma=True)
            units.append((600, gx))
            for h in range(2):
                def finh(s_=s_, st=st, h=h):
                    bank, bkey = pool_slot()
                    A("T", lambda e:
                      e.matmul(bank[:, :], onesbr[:, :],
                               b2bf[0:1, h * 512:(h + 1) * 512], start=True, stop=False),
                      reads=[("onesbr",), ("b2bf",)], writes=[bkey], tag=f"Fb2_{b}_{t}_{s_}")
                    for dc in range(DC):
                        A("T", lambda e, dc=dc:
                          e.matmul(bank[:, :],
                                   xnfm[t % 2][:, dc * 512 + s_ * 128: dc * 512 + (s_ + 1) * 128],
                                   w2sb[:, dc * 1024 + h * 512: dc * 1024 + h * 512 + 512],
                                   start=False, stop=False),
                          reads=[("xnfm", t % 2), ("w", "w2")],
                          writes=[bkey], tag=f"F{b}_{t}_{s_}")
                    for dc in range(DC):
                        A("T", lambda e, dc=dc:
                          e.matmul(bank[:, :],
                                   nfm[:, dc * 2048 + t * 512 + s_ * 128: dc * 2048 + t * 512 + (s_ + 1) * 128],
                                   w2sb[:, (8 + dc) * 1024 + h * 512: (8 + dc) * 1024 + h * 512 + 512],
                                   start=False, stop=False),
                          reads=[("nfm", t), ("w", "w2")], writes=[bkey])
                    for rc in range(2):
                        A("T", lambda e, rc=rc:
                          e.matmul(bank[:, :],
                                   qfm[t % 2][:, rc * 512 + s_ * 128: rc * 512 + (s_ + 1) * 128],
                                   v2sb[:, rc * 1024 + h * 512: rc * 1024 + h * 512 + 512],
                                   start=False, stop=(rc == 1)),
                          reads=[("qfm", t % 2), ("w", "v2")], writes=[bkey])
                    A(FIN_ADD_ENG, lambda e:
                      e.tensor_tensor(out=outsb[st % 2][:, h * 512:(h + 1) * 512],
                                      in0=bank[:, :],
                                      in1=outsb[st % 2][:, h * 512:(h + 1) * 512],
                                      op=AluOpType.add),
                      reads=[bkey, ("outsb", st % 2)], writes=[("outsb", st % 2)])
                units.append((9600, finh))
            def store(st=st):
                A("G", lambda e: e.dma_start(out=out[b, st * 128:(st + 1) * 128, :],
                                             in_=outsb[st % 2][:, :]),
                  reads=[("outsb", st % 2)], chan=("QST", st % 2), dma=True)
            units.append((0, store))
        return units

    def slot4(st):
        i = st % 4
        if i < 2:
            return xa[i], ("xa", i), ("QXA", i)
        return xr[i - 2], ("xr", i - 2), ("QXS", i - 2)

    startup = []
    startup.append(crow_memset_unit(0))
    for st in range(4):
        buf, key, chn = slot4(st)
        startup.extend(stats_units(0, st, buf, key, chn, sum_eng=SUM_ENG)[:1])
    pend_cs = []
    for g in range(NT):
        for s_ in range(4):
            st = 4 * g + s_
            buf, key, chn = slot4(st)
            startup.extend(stats_units(0, st, buf, key, chn, sum_eng=SUM_ENG)[1:])
            if st + 4 < 16:
                b2_, k2_, c2_ = slot4(st + 4)
                startup.extend(stats_units(0, st + 4, b2_, k2_, c2_, sum_eng=SUM_ENG)[:1])
        startup.append((0, lambda g=g: emit_colmath(0, 4 * g, 4 * g + 4)))
        for s_ in range(4):
            st = 4 * g + s_
            rbuf, rkey = outsb[st % 2], ("outsb", st % 2)
            def rld(rbuf=rbuf, rkey=rkey, st=st):
                A("G", lambda e: e.dma_start(out=rbuf[:, :],
                                             in_=x[0, st * 128:(st + 1) * 128, :]),
                  writes=[rkey], chan=("QXF", st % 2), dma=True)
            startup.append((600, rld))
            lc, tp = ph1_units2(0, st, rbuf, rkey, load_chan=None, eng=PH1_ENG, evac="S")
            startup.extend(lc)
            if g == 0:
                plc, ptp = prep_units2(0, st, rbuf, rkey, load_chan=None)
                startup.extend(plc)
                startup.extend(tp)
                startup.extend(ptp)
            else:
                startup.extend(tp)
        for q in range(4):
            startup.append(sim_block_unit(0, q, g, g, 4, g == 3))
            if len(pend_cs) >= 2:
                startup.append(pend_cs.pop(0))
            if g > 0:
                pend_cs.append(cs_unit(0, q, g))
    startup.extend(pend_cs)
    startup.append(crow_unit(0, 0))
    wunits = weight_units()

    play(mix(startup, wunits[:6]))
    wrest = wunits[6:] + [(sum(c for c, _ in wunits[6:]), lambda: None)]

    def batch_steps(b):
        nxt = b + 1
        for sg in range(6):
            if sg == 0:
                continue
            Lsim = sim_tile_units(b, sg) if 1 <= sg <= 3 else []
            if 1 <= sg <= 4:
                Lsim.insert(min(CF_POS, len(Lsim)), cf_unit(b, sg - 1))
            Lffn = ffn_units(b, sg - 1) if 1 <= sg <= 4 else []
            Lfin = fin_units(b, sg - 2) if 2 <= sg <= 4 else []
            Lx = []
            if b + 1 < NB:
                if sg == 2:
                    for st in range(8):
                        buf, key, chn = xa[st % 2], ("xa", st % 2), ("QXA", st % 2)
                        Lx.extend(stats_units(nxt, st, buf, key, chn))
                if sg == 3:
                    for st in range(8, 16):
                        buf, key, chn = xa[st % 2], ("xa", st % 2), ("QXA", st % 2)
                        Lx.extend(stats_units(nxt, st, buf, key, chn))
                    Lx.append((0, lambda: emit_colmath(nxt, 0, 16)))
            Lw = wrest if (b == 0 and sg == 1) else []
            Lp = []
            prep_tps = []
            if sg <= 3:
                for s_ in range(4):
                    st = 4 * sg + s_
                    lc, tp = prep_units2(b, st, xr[st % 2], ("xr", st % 2),
                                         load_chan=("QXR", st % 2), load_eng="G")
                    Lp.extend(lc)
                    prep_tps.extend(tp)
            ph1_tail = []
            if b + 1 < NB and sg == 4:
                per_st = []
                for st in range(12):
                    lc, tp = ph1_units2(nxt, st, xa[st % 2], ("xa", st % 2),
                                        load_chan=("QXA", st % 2), eng="V",
                                        tok=(xntok[st % 4], ("xntok", st % 4)))
                    per_st.append((lc, tp))
                for st in range(4):
                    Lp.extend(per_st[st][0])
                for st in range(12):
                    ph1_tail.extend(per_st[st][1])
                    if st + 4 < 12:
                        ph1_tail.extend(per_st[st + 4][0])
            body = mix(Lsim, Lffn, Lfin, Lx, Lw, Lp)
            play(body)
            play(prep_tps)
            play(ph1_tail)
            if b + 1 < NB and sg == 5:
                play([crow_memset_unit(nxt)])
                secA = []
                ph1_lc3, ph1_tp3 = [], []
                for st in range(12, 16):
                    lc, tp = ph1_units2(nxt, st, xa[st % 2], ("xa", st % 2),
                                        load_chan=("QXA", st % 2), eng="V",
                                        tok=(xntok[st % 4], ("xntok", st % 4)))
                    ph1_lc3.extend(lc)
                    ph1_tp3.extend(tp)
                for kc in range(3):
                    for q in range(4):
                        secA.append(sim_block_unit(nxt, q, kc, kc, 4, False))
                        if kc > 0:
                            secA.append(cs_unit(nxt, q, kc))
                fin3 = fin_units(b, 3)
                play(mix(fin3, secA, ph1_lc3))
                sec_b = list(ph1_tp3)
                p_lc, p_tp = [], []
                for s_ in range(4):
                    lc, tp = prep_units2(nxt, s_, xr[s_ % 2], ("xr", s_ % 2),
                                         load_chan=("QXR", s_ % 2), load_eng="G")
                    p_lc.extend(lc)
                    p_tp.extend(tp)
                sec_b.extend(p_lc)
                for q in range(4):
                    sec_b.append(sim_block_unit(nxt, q, 3, 3, 4, True))
                    sec_b.append(cs_unit(nxt, q, 3))
                sec_b.append(crow_unit(nxt, 0))
                sec_b.extend(p_tp)
                play(sec_b)
            if b + 1 >= NB and sg == 5:
                play(fin_units(b, 3))

    for b in range(NB):
        batch_steps(b)

    sch.finalize()
    globals()["LAST_SCHED"] = sch
    with nc.Block() as block:
        @block.tensor
        def _(eng):
            sch.emit("T", eng, sems)

        @block.vector
        def _(eng):
            sch.emit("V", eng, sems)

        @block.scalar
        def _(eng):
            sch.emit("S", eng, sems)

        @block.gpsimd
        def _(eng):
            sch.emit("G", eng, sems)
            for k in range(3):
                n_store = sch.chan_count.get(("QST", k), 0)
                if n_store:
                    eng.wait_ge(sems[("QST", k)], 16 * n_store)

        @block.sync
        def _(eng):
            sch.emit("SY", eng, sems)
    es.close()
    return nc


def kernel(L0_attn_data, L0_attn_W1, L0_attn_b1, L0_attn_W2,
           L0_ffn_U1, L0_ffn_V1, L0_ffn_b1, L0_ffn_U2, L0_ffn_V2, L0_ffn_b2):
    n = 8
    def h(a):
        a = np.asarray(a)
        assert a.dtype == np.float32, a.dtype
        return a
    xfull = np.ascontiguousarray(h(L0_attn_data))
    com = {
        "W1": np.ascontiguousarray(h(L0_attn_W1).reshape(8, 128).T),
        "b1": np.ascontiguousarray(h(L0_attn_b1).reshape(8, 128).T),
        "W2": np.ascontiguousarray(h(L0_attn_W2)),
        "U1": np.ascontiguousarray(h(L0_ffn_U1)),
        "V1": np.ascontiguousarray(h(L0_ffn_V1)),
        "fb1": np.ascontiguousarray(h(L0_ffn_b1).reshape(32, 128).T),
        "U2": np.ascontiguousarray(h(L0_ffn_U2)),
        "V2": np.ascontiguousarray(h(L0_ffn_V2)),
        "b2": np.ascontiguousarray(h(L0_ffn_b2).reshape(1, 1024)),
        "idb": np.eye(128, dtype=np.float32).astype(ml_dtypes.bfloat16),
        "idf": np.eye(128, dtype=np.float32),
    }
    nc = build_program()
    in_maps = [dict(com, x=np.ascontiguousarray(xfull[2 * c:2 * c + 2])) for c in range(n)]
    res = run_bass_kernel_spmd(nc, in_maps, core_ids=list(range(n)))
    return np.concatenate([res.results[c]["out"] for c in range(n)], axis=0)
